# version 05
# baseline (speedup 1.0000x reference)
import contextlib

import numpy as np

import concourse.bass as bass
import concourse.mybir as mybir
from concourse.bass_utils import run_bass_kernel_spmd
from concourse import library_config
from bass_rust import ReduceOp

F32 = mybir.dt.float32
BF16 = mybir.dt.bfloat16
I32 = mybir.dt.int32
ALU = mybir.AluOpType
ACTF = mybir.ActivationFunctionType
AX = mybir.AxisListType

T = 2048
D = 1024
CH = D // 128
XST = T + 1
NEG = -1.0e9
STAGE = 6


class Step:
    __slots__ = ("stream", "idx")

    def __init__(self, stream, idx):
        self.stream = stream
        self.idx = idx


class Sched:
    STREAMS = {
        "PE": ("tensor", 1),
        "DVE": ("vector", 1),
        "ACT": ("scalar", 1),
        "GPC": ("gpsimd", 1),
        "SD": ("sync", 16),
        "GD": ("gpsimd", 16),
    }
    ENGINES = ("tensor", "vector", "scalar", "gpsimd", "sync")
    DRAINABLE = {"DVE", "ACT", "GPC"}

    def __init__(self):
        self.recs = {s: [] for s in self.STREAMS}
        self.order = {e: [] for e in self.ENGINES}
        self.last_drain = {s: -1 for s in self.STREAMS}
        self.mem_w = {}
        self.mem_r = {}

    def step(self, stream, emit, reads=(), writes=(), deps=(), tag=""):
        dep_list = list(deps)
        for k in reads:
            w = self.mem_w.get(k)
            if w is not None:
                dep_list.append(w)
        for k in writes:
            dep_list.extend(self.mem_r.pop(k, ()))
            w = self.mem_w.get(k)
            if w is not None:
                dep_list.append(w)
        idx = len(self.recs[stream])
        me = Step(stream, idx)
        eng = self.STREAMS[stream][0]
        drain = False
        xdeps = {}
        for d in dep_list:
            if d is None or (d.stream == stream and d.idx >= idx):
                continue
            if d.stream == stream:
                if stream in self.DRAINABLE:
                    if d.idx > self.last_drain[stream]:
                        drain = True
                elif stream in ("SD", "GD"):
                    xdeps[d.stream] = max(xdeps.get(d.stream, -1), d.idx)
            else:
                xdeps[d.stream] = max(xdeps.get(d.stream, -1), d.idx)
        if drain:
            self.last_drain[stream] = idx - 1
        for k in reads:
            self.mem_r.setdefault(k, []).append(me)
        for k in writes:
            self.mem_w[k] = me
        self.recs[stream].append({"emit": emit, "xdeps": xdeps, "drain": drain,
                                  "tag": tag})
        self.order[eng].append((stream, idx))
        return me

    def pe(self, emit, **kw):
        return self.step("PE", emit, **kw)

    def dve(self, emit, **kw):
        return self.step("DVE", emit, **kw)

    def act(self, emit, **kw):
        return self.step("ACT", emit, **kw)

    def gpc(self, emit, **kw):
        return self.step("GPC", emit, **kw)

    def dma(self, out, in_, stream="SD", slow=False, **kw):
        return self.step(stream, lambda e: e.dma_start(
            out=out, in_=in_, allow_slow_non_contiguous=slow), **kw)

    POOL = {"SD": 48, "GD": 24}

    def finalize(self, block, sems, pools):
        need = {s: [False] * len(self.recs[s]) for s in self.STREAMS}
        for s in self.STREAMS:
            for rec in self.recs[s]:
                for ds, di in rec["xdeps"].items():
                    need[ds][di] = True
        for s in self.STREAMS:
            if self.recs[s]:
                need[s][-1] = True
        pref = {}
        for s in ("PE", "DVE", "ACT", "GPC"):
            c, ps = 0, []
            for f in need[s]:
                c += 1 if f else 0
                ps.append(c)
            pref[s] = ps
        dma_slot = {}
        for s in ("SD", "GD"):
            np_ = self.POOL[s]
            uses = [0] * np_
            for i in range(len(self.recs[s])):
                sl = i % np_
                uses[sl] += 1
                dma_slot[(s, i)] = (sl, uses[sl])
            pref[s] = uses

        def wait_of(ds, di):
            if ds in ("SD", "GD"):
                sl, use = dma_slot[(ds, di)]
                return (ds, sl), pools[ds][sl], 16 * use
            return (ds, 0), sems[ds], pref[ds][di]

        def run_engine(engname, eng):
            waited = {}
            for (stream, idx) in self.order[engname]:
                rec = self.recs[stream][idx]
                for ds, di in sorted(rec["xdeps"].items()):
                    key, sem, val = wait_of(ds, di)
                    if val > waited.get(key, 0):
                        eng.wait_ge(sem, val)
                        waited[key] = val
                if rec["drain"]:
                    eng.drain()
                if stream in ("SD", "GD"):
                    sl, use = dma_slot[(stream, idx)]
                    if use > 1:
                        key = (stream, sl)
                        val = 16 * (use - 1)
                        if val > waited.get(key, 0):
                            eng.wait_ge(pools[stream][sl], val)
                            waited[key] = val
                    inst = rec["emit"](eng)
                    inst.then_inc(pools[stream][sl], 16)
                else:
                    inst = rec["emit"](eng)
                    if need[stream][idx]:
                        if inst is None:
                            raise RuntimeError(
                                f"step {stream}:{idx} ({rec['tag']}) targeted "
                                "but no inst returned")
                        inst.then_inc(sems[stream], 1)

        @block.tensor
        def _(t):
            run_engine("tensor", t)

        @block.vector
        def _(v):
            run_engine("vector", v)

        @block.scalar
        def _(s):
            run_engine("scalar", s)

        @block.gpsimd
        def _(g):
            run_engine("gpsimd", g)
            for s in ("PE", "DVE", "ACT"):
                tot = pref[s][-1] if self.recs[s] else 0
                if tot:
                    g.wait_ge(sems[s], tot)
            for s in ("SD", "GD"):
                for sl, cnt in enumerate(pref[s]):
                    if cnt:
                        g.wait_ge(pools[s][sl], 16 * cnt)

        @block.sync
        def _(sy):
            run_engine("sync", sy)


def build():
    nc = bass.Bass("TRN2", target_bir_lowering=False)

    x_in = nc.dram_tensor("x", [T, D], F32, kind="ExternalInput").ap()
    y_out = nc.dram_tensor("y", [T, D], F32, kind="ExternalOutput").ap()
    wp = {}
    for n, shp in [("L0f_Wr", [D, D]), ("L0f_Wk", [D, D]), ("L0f_Wv", [D, D]),
                   ("L0a_Wq", [D, D]), ("L0a_Wk", [D, D]), ("L0a_Wv", [D, D]),
                   ("L0a_Wo", [D, D]), ("L1a_W", [3 * D, D]), ("L1a_Wo", [D, D]),
                   ("L1f_Wg", [4096, D]), ("L1f_Wu", [4096, D]),
                   ("L1f_Wd", [D, 4096])]:
        wp[n] = nc.dram_tensor(n, shp, F32, kind="ExternalInput").ap()
    vp = {}
    for n, shp in [("mu_r", [D]), ("mu_k", [D]), ("b_attn", [3 * D]), ("b_o", [D])]:
        vp[n] = nc.dram_tensor(n, shp, F32, kind="ExternalInput").ap()

    scr = {}
    for n, (nin, nout) in [("L0f_Wr", (D, D)), ("L0f_Wk", (D, D)),
                           ("L0f_Wv", (D, D)), ("L0a_Wq", (D, D)),
                           ("L0a_Wk", (D, D)), ("L0a_Wv", (D, D)),
                           ("L0a_Wo", (D, D)), ("L1a_W", (D, 3 * D)),
                           ("L1a_Wo", (D, D))]:
        scr[n] = nc.dram_tensor("s_" + n, [nout // 128, nin, 128], F32).ap()
    for n, (nin, nout) in [("L1f_Wg", (D, 4096)), ("L1f_Wu", (D, 4096)),
                           ("L1f_Wd", (4096, D))]:
        scr[n] = nc.dram_tensor("s_" + n, [nout // 128, nin, 128], BF16).ap()

    xres = nc.alloc_sbuf_tensor("xres", [128, CH * T], F32)
    xln = nc.alloc_sbuf_tensor("xln", [128, CH * XST], F32)
    AR = 17664
    arena = nc.alloc_sbuf_tensor("arena", [128, AR], F32)
    stats = nc.alloc_sbuf_tensor("stats", [128, 512], F32)
    ident = nc.alloc_sbuf_tensor("ident", [128, 128], F32)
    onescol = nc.alloc_sbuf_tensor("onescol", [128, 128], F32)
    muv = nc.alloc_sbuf_tensor("muv", [128, 4 * CH], F32)
    biasv = nc.alloc_sbuf_tensor("biasv", [128, 48], F32)
    masks = nc.alloc_sbuf_tensor("masks", [128, 4 * 512], BF16)
    iotw = nc.alloc_sbuf_tensor("iotw", [128, 512], F32)
    identb = nc.alloc_sbuf_tensor("identb", [128, 128], BF16)
    pb = [nc.alloc_psum_tensor(f"pb{i}", [128, 512], F32) for i in range(8)]
    xqv = xln[:].bitcast(BF16)

    S = Sched()

    def cell_keys(off, size):
        return [f"ac{j}" for j in range(off // 2048, (off + size - 1) // 2048 + 1)]

    def A(off, size):
        return arena[:, off:off + size], cell_keys(off, size)

    def AB(off_f32, size_bf16):
        v = arena[:].bitcast(BF16)
        return (v[:, 2 * off_f32:2 * off_f32 + size_bf16],
                cell_keys(off_f32, (size_bf16 + 1) // 2))

    SC = {"Wg_m": 200, "Wu_m": 202, "Wd_m": 204,
          "Wg_s": 201, "Wu_s": 203, "Wd_s": 205,
          "bx": 208,
          "ag": 226, "au": 227, "gm": 232, "sh": 233, "bh": 234,
          "mA": 240, "mB": 241, "nA": 242, "nB": 243,
          "zA": 244, "zB": 245, "zt": 246, "rA": 248, "rB": 249}

    S.gpc(lambda g: g.iota(iotw[:], pattern=[[1, 512]], channel_multiplier=-1,
                           allow_small_or_imprecise_dtypes=True),
          writes=["iotw"], tag="iota")
    S.dve(lambda v: v.tensor_scalar(out=ident[:], in0=iotw[:, 0:128],
                                    scalar1=0.0, scalar2=None, op0=ALU.is_equal),
          reads=["iotw"], writes=["ident"], tag="ident")
    S.dve(lambda v: v.tensor_copy(out=identb[:], in_=ident[:]),
          reads=["ident"], writes=["identb"], tag="identb")
    S.dve(lambda v: v.memset(onescol[:], 1.0), writes=["onescol"], tag="ones")

    def emit_masks():
        for vv in range(4):
            S.dve(lambda v, vv=vv: v.tensor_scalar(
                out=masks[:, 512 * vv:512 * vv + 512], in0=iotw[:],
                scalar1=float(128 * vv), scalar2=NEG, op0=ALU.is_gt,
                op1=ALU.mult),
                reads=["iotw"], writes=["masks"], tag=f"mask{vv}")
    emit_masks()

    S.dma(muv[:, 0:CH], vp["mu_r"].rearrange("(c p) -> p c", p=128),
          writes=["muv"], slow=True, tag="ld_mur")
    S.dma(muv[:, 2 * CH:3 * CH], vp["mu_k"].rearrange("(c p) -> p c", p=128),
          writes=["muv"], slow=True, tag="ld_muk")
    S.dve(lambda v: v.tensor_scalar(out=muv[:, CH:2 * CH], in0=muv[:, 0:CH],
                                    scalar1=-1.0, scalar2=1.0, op0=ALU.mult,
                                    op1=ALU.add),
          reads=["muv"], writes=["muv"], tag="mur1")
    S.dve(lambda v: v.tensor_scalar(out=muv[:, 3 * CH:4 * CH],
                                    in0=muv[:, 2 * CH:3 * CH], scalar1=-1.0,
                                    scalar2=1.0, op0=ALU.mult, op1=ALU.add),
          reads=["muv"], writes=["muv"], tag="muk1")
    S.dma(biasv[:, 0:24], vp["b_attn"].rearrange("(c p) -> p c", p=128),
          writes=["biasv"], slow=True, tag="ld_battn")
    S.dma(biasv[:, 24:32], vp["b_o"].rearrange("(c p) -> p c", p=128),
          writes=["biasv"], slow=True, tag="ld_bo")
    S.dve(lambda v: v.tensor_scalar(out=biasv[:, 32:40], in0=biasv[:, 8:16],
                                    scalar1=0.125, scalar2=None, op0=ALU.mult),
          reads=["biasv"], writes=["biasv"], tag="bq")

    xin_t = x_in.rearrange("(k p) d -> k p d", p=128)
    for k in range(16):
        sl, slk = A(1024 * (k % 2), 1024)
        S.dma(sl, xin_t[k], writes=slk, tag=f"ld_x{k}")
        for c in range(CH):
            bk = c % 2
            S.pe(lambda t, sl=sl, c=c, bk=bk: t.transpose(
                pb[bk][:, 0:128], sl[:, 128 * c:128 * c + 128], ident[:]),
                reads=slk + ["ident"], writes=[f"pb{bk}"], tag=f"tpx{k}_{c}")
            S.dve(lambda v, c=c, k=k, bk=bk: v.tensor_copy(
                out=xres[:, T * c + 128 * k: T * c + 128 * k + 128],
                in_=pb[bk][:, 0:128]),
                reads=[f"pb{bk}"], writes=[f"xres:{c}"], tag=f"evx{k}_{c}")
    for c in range(CH):
        S.dve(lambda v, c=c: v.memset(xln[:, XST * c:XST * c + 1], 0.0),
              writes=[f"xlnz:{c}"], tag=f"xlnz{c}")

    def prep_f32(name):
        nout, nin = wp[name].shape
        w_t = wp[name].rearrange("(b p) i -> b p i", p=128)
        for b in range(nout // 128):
            sl, slk = A(4096 + (b % 2) * nin, nin)
            pn, pnk = A(4096 + 2 * nin + (b % 2) * nin, nin)
            S.dma(sl, w_t[b], writes=slk, tag=f"pwL{name}{b}")
            for c in range(nin // 128):
                bk = 6 + c % 2
                S.pe(lambda t, sl=sl, c=c, bk=bk: t.transpose(
                    pb[bk][:, 0:128], sl[:, 128 * c:128 * c + 128], ident[:]),
                    reads=slk + ["ident"], writes=[f"pb{bk}"],
                    tag=f"pwT{name}{b}_{c}")
                S.dve(lambda v, pn=pn, c=c, bk=bk: v.tensor_copy(
                    out=pn[:, 128 * c:128 * c + 128], in_=pb[bk][:, 0:128]),
                    reads=[f"pb{bk}"], writes=pnk, tag=f"pwE{name}{b}_{c}")
            dst = scr[name][b].rearrange("(c q) f -> q c f", q=128)
            src = pn.rearrange("p (c f) -> p c f", c=nin // 128)
            S.dma(dst, src, stream="GD", reads=pnk,
                  writes=[f"scr:{name}:{b}"], tag=f"pwS{name}{b}")

    def prep_bitlinear(name):
        nout, nin = wp[name].shape
        w_t = wp[name].rearrange("(b p) i -> b p i", p=128)
        nslab, nch = nout // 128, nin // 128
        mc, sc2 = SC[name[4:] + "_m"], SC[name[4:] + "_s"]
        acc, acck = A(12288, 128)
        for b in range(nslab):
            sl, slk = A(4096 + (b % 2) * nin, nin)
            S.dma(sl, w_t[b], writes=slk, tag=f"pqL1{name}{b}")
            red, redk = A(12416 + (b % 2) * 128, 128)
            S.dve(lambda v, sl=sl, red=red: v.tensor_reduce(
                out=red[:, 0:1], in_=sl, axis=AX.X, op=ALU.add,
                apply_absolute_value=True),
                reads=slk, writes=redk, tag=f"pqR{name}{b}")
            if b == 0:
                S.dve(lambda v, red=red: v.tensor_copy(out=acc[:, 0:1],
                                                       in_=red[:, 0:1]),
                      reads=redk, writes=acck, tag=f"pqA{name}{b}")
            else:
                S.dve(lambda v, red=red: v.tensor_tensor(
                    out=acc[:, 0:1], in0=acc[:, 0:1], in1=red[:, 0:1],
                    op=ALU.add),
                    reads=redk + acck, writes=acck, tag=f"pqA{name}{b}")
        S.pe(lambda t: t.matmul(pb[6][0:1, 0:1], onescol[:, 0:1], acc[:, 0:1],
                                start=True, stop=True),
             reads=acck + ["onescol"], writes=["pb6"], tag=f"pqP{name}")
        S.act(lambda s: s.activation(out=stats[0:1, mc:mc + 1],
                                     in_=pb[6][0:1, 0:1], func=ACTF.Copy,
                                     scale=1.0 / (nout * nin)),
              reads=["pb6"], writes=[f"st:{mc}"], tag=f"pqM{name}")
        S.dve(lambda v: v.tensor_scalar(out=stats[0:1, mc:mc + 1],
                                        in0=stats[0:1, mc:mc + 1], scalar1=1e-5,
                                        scalar2=None, op0=ALU.max),
              reads=[f"st:{mc}"], writes=[f"st:{mc}"], tag=f"pqC{name}")
        S.dve(lambda v: v.reciprocal(out=stats[0:1, sc2:sc2 + 1],
                                     in_=stats[0:1, mc:mc + 1]),
              reads=[f"st:{mc}"], writes=[f"st:{sc2}"], tag=f"pqI{name}")
        S.dve(lambda v: v.tensor_scalar(out=stats[0:1, mc:mc + 1],
                                        in0=stats[0:1, mc:mc + 1],
                                        scalar1=1.0 / 127.0, scalar2=None,
                                        op0=ALU.mult),
              reads=[f"st:{mc}"], writes=[f"st:{mc}"], tag=f"pqS{name}")
        S.gpc(lambda g: g.partition_broadcast(stats[:, mc:mc + 1],
                                              stats[0:1, mc:mc + 1]),
              reads=[f"st:{mc}", "lib:mlp"], writes=[f"st:{mc}"],
              tag=f"pqB1{name}")
        S.gpc(lambda g: g.partition_broadcast(stats[:, sc2:sc2 + 1],
                                              stats[0:1, sc2:sc2 + 1]),
              reads=[f"st:{sc2}", "lib:mlp"], writes=[f"st:{sc2}"],
              tag=f"pqB2{name}")
        for b in range(nslab):
            sl, slk = A(4096 + (b % 2) * nin, nin)
            t1, t1k = A(4096 + 2 * nin + (b % 2) * nin, nin) if nin <= 1024 \
                else A(12544, nin)
            qs, qsk = AB(12544 + 2 * 1024 + (b % 2) * (nin // 2), nin) if nin <= 1024 \
                else AB(4096 + (b % 2) * (nin // 2), nin)
            pn, pnk = AB(12544 + 3 * 1024 + (b % 2) * (nin // 2), nin) if nin <= 1024 \
                else AB(6144 + (b % 2) * (nin // 2), nin)
            S.dma(sl, w_t[b], writes=slk, tag=f"pqL2{name}{b}")
            S.dve(lambda v, sl=sl, t1=t1: v.tensor_scalar(
                out=t1, in0=sl, scalar1=stats[:, sc2:sc2 + 1], scalar2=0.5,
                op0=ALU.mult, op1=ALU.add),
                reads=slk + [f"st:{sc2}"] + t1k, writes=t1k, tag=f"pq1{name}{b}")
            S.dve(lambda v, sl=sl, t1=t1: v.tensor_copy(
                out=sl.bitcast(I32), in_=t1),
                reads=t1k + slk, writes=slk, tag=f"pq2{name}{b}")
            S.dve(lambda v, sl=sl: v.tensor_copy(
                out=sl, in_=sl.bitcast(I32)),
                reads=slk, writes=slk, tag=f"pq2b{name}{b}")
            S.dve(lambda v, sl=sl, t1=t1: v.tensor_tensor(
                out=t1, in0=sl, in1=t1, op=ALU.subtract),
                reads=t1k + slk, writes=t1k, tag=f"pq3{name}{b}")
            S.dve(lambda v, t1=t1: v.tensor_scalar(
                out=t1, in0=t1, scalar1=0.0, scalar2=None, op0=ALU.is_gt),
                reads=t1k, writes=t1k, tag=f"pq3b{name}{b}")
            S.dve(lambda v, sl=sl, t1=t1: v.tensor_tensor(
                out=t1, in0=sl, in1=t1, op=ALU.subtract),
                reads=t1k + slk, writes=t1k, tag=f"pq3c{name}{b}")
            S.dve(lambda v, t1=t1, qs=qs: v.tensor_scalar(
                out=qs, in0=t1, scalar1=1.0, scalar2=-1.0,
                op0=ALU.min, op1=ALU.max),
                reads=t1k + qsk, writes=qsk, tag=f"pq4{name}{b}")
            for c in range(nch):
                bk = 6 + c % 2
                S.pe(lambda t, qs=qs, c=c, bk=bk: t.transpose(
                    pb[bk][:, 0:64].bitcast(BF16)[:, 0:128],
                    qs[:, 128 * c:128 * c + 128], identb[:]),
                    reads=qsk + ["identb"], writes=[f"pb{bk}"],
                    tag=f"pqT{name}{b}_{c}")
                S.dve(lambda v, pn=pn, c=c, bk=bk: v.tensor_copy(
                    out=pn[:, 128 * c:128 * c + 128],
                    in_=pb[bk][:, 0:64].bitcast(BF16)[:, 0:128]),
                    reads=[f"pb{bk}"] + pnk, writes=pnk, tag=f"pqE{name}{b}_{c}")
            dst = scr[name][b].rearrange("(c q) f -> q c f", q=128)
            src = pn.rearrange("p (c f) -> p c f", c=nch)
            S.dma(dst, src, stream="GD", reads=pnk,
                  writes=[f"scr:{name}:{b}"], tag=f"pqS{name}{b}")

    def emit_ln(ph):
        abuf, abk = A(14336, 512)
        bbuf, bbk = A(14848, 512)
        t1, t1k = A(15360, 512)
        t2, t2k = A(15872, 512)
        sq = [A(16384, 512), A(16896, 512)]
        for n in range(4):
            for c in range(CH):
                mv = xres[:, T * c + 512 * n: T * c + 512 * n + 512]
                S.pe(lambda t, mv=mv, c=c: t.matmul(
                    pb[6][:], onescol[:], mv, start=(c == 0), stop=(c == CH - 1)),
                    reads=[f"xres:{c}", "onescol"], writes=["pb6"],
                    tag=f"ln{ph}s{n}_{c}")
            for c in range(CH):
                mv = xres[:, T * c + 512 * n: T * c + 512 * n + 512]
                sqb, sqk = sq[c % 2]
                S.act(lambda s, sqb=sqb, mv=mv: s.activation(
                    out=sqb, in_=mv, func=ACTF.Square),
                    reads=[f"xres:{c}"] + sqk, writes=sqk, tag=f"ln{ph}q{n}_{c}")
                S.pe(lambda t, sqb=sqb, c=c: t.matmul(
                    pb[7][:], onescol[:], sqb, start=(c == 0), stop=(c == CH - 1)),
                    reads=sqk + ["onescol"], writes=["pb7"],
                    tag=f"ln{ph}p{n}_{c}")
            S.dve(lambda v: v.tensor_scalar(out=abuf, in0=pb[6][:],
                                            scalar1=1.0 / D, scalar2=None,
                                            op0=ALU.mult),
                  reads=["pb6"] + abk, writes=abk, tag=f"ln{ph}mu{n}")
            S.dve(lambda v: v.tensor_scalar(out=t1, in0=pb[7][:],
                                            scalar1=1.0 / D, scalar2=None,
                                            op0=ALU.mult),
                  reads=["pb7"] + t1k, writes=t1k, tag=f"ln{ph}e{n}")
            S.dve(lambda v: v.tensor_tensor(out=t2, in0=abuf, in1=abuf,
                                            op=ALU.mult),
                  reads=abk + t2k, writes=t2k, tag=f"ln{ph}f{n}")
            S.dve(lambda v: v.scalar_tensor_tensor(out=t2, in0=t1, scalar=1e-5,
                                                   in1=t2, op0=ALU.add,
                                                   op1=ALU.subtract),
                  reads=t1k + t2k, writes=t2k, tag=f"ln{ph}h{n}")
            S.act(lambda s: s.activation(out=bbuf, in_=t2, func=ACTF.Sqrt),
                  reads=t2k + bbk, writes=bbk, tag=f"ln{ph}r{n}")
            S.dve(lambda v: v.reciprocal(out=t1, in_=bbuf),
                  reads=bbk + t1k, writes=t1k, tag=f"ln{ph}c{n}")
            S.dve(lambda v: v.tensor_tensor(out=bbuf, in0=t1, in1=t1,
                                            op=ALU.mult),
                  reads=t1k + bbk, writes=bbk, tag=f"ln{ph}n1{n}")
            S.dve(lambda v: v.tensor_tensor(out=bbuf, in0=t2, in1=bbuf,
                                            op=ALU.mult),
                  reads=bbk + t2k, writes=bbk, tag=f"ln{ph}n2{n}")
            S.dve(lambda v: v.tensor_scalar(out=bbuf, in0=bbuf, scalar1=-0.5,
                                            scalar2=1.5, op0=ALU.mult,
                                            op1=ALU.add),
                  reads=bbk, writes=bbk, tag=f"ln{ph}n3{n}")
            S.dve(lambda v: v.tensor_tensor(out=t2, in0=t1, in1=bbuf,
                                            op=ALU.mult),
                  reads=bbk + t1k + t2k, writes=t2k, tag=f"ln{ph}a{n}")
            S.dve(lambda v: v.tensor_tensor(out=t1, in0=abuf, in1=t2,
                                            op=ALU.mult),
                  reads=abk + t2k + t1k, writes=t1k, tag=f"ln{ph}b{n}")
            for c in range(CH):
                src = xres[:, T * c + 512 * n: T * c + 512 * n + 512]
                dst = xln[:, XST * c + 1 + 512 * n: XST * c + 1 + 512 * n + 512]
                S.dve(lambda v, src=src: v.tensor_tensor(
                    out=abuf, in0=src, in1=t2, op=ALU.mult),
                    reads=t2k + [f"xres:{c}"] + abk, writes=abk,
                    tag=f"ln{ph}x{n}_{c}")
                S.dve(lambda v, dst=dst: v.tensor_tensor(
                    out=dst, in0=abuf, in1=t1, op=ALU.subtract),
                    reads=abk + t1k, writes=[f"xln:{c}:{n}"],
                    tag=f"ln{ph}o{n}_{c}")

    FT = 512
    NFT = T // FT

    def emit_ffn0(ph):
        emit_ln(ph)
        mixb, mixk = A(0, CH * FT)
        rbuf, rbk = A(4096, CH * FT)
        kbuf, kbk = A(8192, CH * FT)
        wsl = [A(12288, 1024), A(13312, 1024)]

        def load_panel(wname, d, slot, tg):
            dst, dk = wsl[slot]
            pan = scr[wname][d].rearrange("(c q) f -> q c f", q=128)
            dv = dst.rearrange("p (c f) -> p c f", c=CH)
            S.dma(dv, pan, reads=[f"scr:{wname}:{d}"], writes=dk, tag=tg)
            return dk

        def xln_reads(c, t0):
            ks = [f"xln:{c}:{t0 // 512}", f"xlnz:{c}"]
            if t0 > 0:
                ks.append(f"xln:{c}:{(t0 - 1) // 512}")
            return ks

        for ti in range(NFT):
            t0 = ti * FT

            def emit_mix(mu0, mu1, tg):
                for c in range(CH):
                    sh = xln[:, XST * c + t0: XST * c + t0 + FT]
                    un = xln[:, XST * c + 1 + t0: XST * c + 1 + t0 + FT]
                    mb = mixb[:, FT * c:FT * c + FT]
                    S.dve(lambda v, sh=sh, mb=mb, mu1=mu1, c=c: v.tensor_scalar(
                        out=mb, in0=sh, scalar1=muv[:, mu1 + c:mu1 + c + 1],
                        scalar2=None, op0=ALU.mult),
                        reads=xln_reads(c, t0) + ["muv"] + mixk, writes=mixk,
                        tag=f"{tg}1_{c}")
                    S.dve(lambda v, un=un, mb=mb, mu0=mu0, c=c:
                          v.scalar_tensor_tensor(
                              out=mb, in0=un, scalar=muv[:, mu0 + c:mu0 + c + 1],
                              in1=mb, op0=ALU.mult, op1=ALU.add),
                          reads=xln_reads(c, t0) + ["muv"] + mixk, writes=mixk,
                          tag=f"{tg}2_{c}")

            def emit_mm_block(wname, mvbuf, mvkeys, outfn, tg):
                for d in range(CH):
                    dk = load_panel(wname, d, d % 2, f"{tg}L{d}")
                    bank = 2 + d % 2

                    def group(t, d=d, bank=bank, mvbuf=mvbuf):
                        sl, _ = wsl[d % 2]
                        last = None
                        for c in range(CH):
                            last = t.matmul(
                                pb[bank][:], sl[:, 128 * c:128 * c + 128],
                                mvbuf[:, FT * c:FT * c + FT],
                                start=(c == 0), stop=(c == CH - 1))
                        return last
                    S.pe(group, reads=dk + list(mvkeys), writes=[f"pb{bank}"],
                         tag=f"{tg}M{d}")
                    outfn(d, bank)

            emit_mix(0, CH, f"f0{ph}xr{ti}")

            def evac_r(d, bank):
                S.act(lambda s, d=d, bank=bank: s.activation(
                    out=rbuf[:, FT * d:FT * d + FT], in_=pb[bank][:],
                    func=ACTF.Sigmoid),
                    reads=[f"pb{bank}"] + rbk, writes=rbk,
                    tag=f"f0{ph}sg{ti}_{d}")
            emit_mm_block("L0f_Wr", mixb, mixk, evac_r, f"f0{ph}r{ti}")

            emit_mix(2 * CH, 3 * CH, f"f0{ph}xk{ti}")

            def evac_k(d, bank):
                S.dve(lambda v, d=d, bank=bank: v.tensor_scalar(
                    out=kbuf[:, FT * d:FT * d + FT], in0=pb[bank][:],
                    scalar1=0.0, scalar2=None, op0=ALU.max),
                    reads=[f"pb{bank}"] + kbk, writes=kbk,
                    tag=f"f0{ph}re{ti}_{d}")
                S.act(lambda s, d=d: s.activation(
                    out=kbuf[:, FT * d:FT * d + FT],
                    in_=kbuf[:, FT * d:FT * d + FT], func=ACTF.Square),
                    reads=kbk, writes=kbk, tag=f"f0{ph}sq{ti}_{d}")
            emit_mm_block("L0f_Wk", mixb, mixk, evac_k, f"f0{ph}k{ti}")

            def evac_v(d, bank):
                S.dve(lambda v, d=d, bank=bank: v.tensor_tensor(
                    out=pb[bank][:], in0=pb[bank][:],
                    in1=rbuf[:, FT * d:FT * d + FT], op=ALU.mult),
                    reads=[f"pb{bank}"] + rbk, writes=[f"pb{bank}"],
                    tag=f"f0{ph}mu{ti}_{d}")
                S.dve(lambda v, d=d, bank=bank, t0=t0: v.scalar_tensor_tensor(
                    out=xres[:, T * d + t0: T * d + t0 + FT], in0=pb[bank][:],
                    scalar=0.5, in1=xres[:, T * d + t0: T * d + t0 + FT],
                    op0=ALU.mult, op1=ALU.add),
                    reads=[f"pb{bank}", f"xres:{d}"], writes=[f"xres:{d}"],
                    tag=f"f0{ph}rs{ti}_{d}")
            emit_mm_block("L0f_Wv", kbuf, kbk, evac_v, f"f0{ph}v{ti}")

    def emit_attn(ph, layer):
        emit_ln(ph)
        qb, qbk = A(0, 2048)
        kb, kbk2 = A(2048, 2048)
        vb, vbk = A(4096, 2048)
        vt, vtk = A(6144, 2048)
        pbf, pbk2 = A(8192, 2048)
        pta, ptak = A(10240, 2048)
        ptb, ptbk = A(12288, 2048)
        wsl = [A(14336, 1024), A(15360, 1024)]

        if layer == 0:
            pn_q, pn_k, pn_v, wo = "L0a_Wq", "L0a_Wk", "L0a_Wv", "L0a_Wo"
            qoff = koff = voff = 0
        else:
            pn_q = pn_k = pn_v = "L1a_W"
            koff, qoff, voff, wo = 0, 8, 16, "L1a_Wo"

        for p in range(CH):
            def load_pair(wname, blk, slot, tg):
                dst, dk = wsl[slot]
                pan = scr[wname][blk].rearrange("(c q) f -> q c f", q=128)
                dv = dst.rearrange("p (c f) -> p c f", c=CH)
                S.dma(dv, pan, reads=[f"scr:{wname}:{blk}"], writes=dk, tag=tg)
                return dk

            def emit_proj(wname, blk, slot, dstb, dstk, evac, tg):
                dk = load_pair(wname, blk, slot, f"{tg}L")
                for n in range(4):
                    bank = 2 + n % 2

                    def group(t, bank=bank, n=n, slot=slot):
                        sl, _ = wsl[slot]
                        last = None
                        for c in range(CH):
                            last = t.matmul(
                                pb[bank][:], sl[:, 128 * c:128 * c + 128],
                                xln[:, XST * c + 1 + 512 * n:
                                     XST * c + 1 + 512 * n + 512],
                                start=(c == 0), stop=(c == CH - 1))
                        return last
                    S.pe(group,
                         reads=dk + [f"xln:{c}:{n}" for c in range(CH)],
                         writes=[f"pb{bank}"], tag=f"{tg}M{n}")
                    evac(n, bank, dstb, dstk)

            def ev_q(n, bank, dstb, dstk):
                kw = {} if layer == 0 else {"bias": biasv[:, 32 + p:33 + p]}
                S.act(lambda s, n=n, bank=bank, kw=kw: s.activation(
                    out=dstb[:, 512 * n:512 * n + 512], in_=pb[bank][:],
                    func=ACTF.Identity, scale=0.125, **kw),
                    reads=[f"pb{bank}", "biasv"] + dstk, writes=dstk,
                    tag=f"a{ph}eq{p}_{n}")

            def ev_plain(bcol):
                def f(n, bank, dstb, dstk):
                    kw = {} if layer == 0 else {"bias": biasv[:, bcol:bcol + 1]}
                    S.act(lambda s, n=n, bank=bank, kw=kw: s.activation(
                        out=dstb[:, 512 * n:512 * n + 512], in_=pb[bank][:],
                        func=ACTF.Identity, **kw),
                        reads=[f"pb{bank}", "biasv"] + dstk, writes=dstk,
                        tag=f"a{ph}ekv{p}_{n}")
                return f

            emit_proj(pn_q, qoff + p, 0, qb, qbk, ev_q, f"a{ph}q{p}")
            emit_proj(pn_k, koff + p, 1, kb, kbk2, ev_plain(0 + p), f"a{ph}k{p}")
            emit_proj(pn_v, voff + p, 0, vb, vbk, ev_plain(16 + p), f"a{ph}v{p}")

            for j in range(16):
                bank = 4 + j % 2
                S.pe(lambda t, j=j, bank=bank: t.transpose(
                    pb[bank][:, 0:128], vb[:, 128 * j:128 * j + 128], ident[:]),
                    reads=vbk + ["ident"], writes=[f"pb{bank}"],
                    tag=f"a{ph}vt{p}_{j}")
                S.dve(lambda v, j=j, bank=bank: v.tensor_copy(
                    out=vt[:, 128 * j:128 * j + 128], in_=pb[bank][:, 0:128]),
                    reads=[f"pb{bank}"] + vtk, writes=vtk, tag=f"a{ph}ve{p}_{j}")

            for i in range(16):
                nb = i // 4 + 1
                for h in range(2):
                    hk = "AB"[h]
                    base = 64 * h
                    tp = None if h == 0 else (64, 0)
                    for b in range(nb):
                        bank = 4 * h + b
                        S.pe(lambda t, b=b, base=base, i=i, bank=bank, tp=tp:
                             t.matmul(pb[bank][:],
                                      qb[base:base + 64, 128 * i:128 * i + 128],
                                      kb[base:base + 64, 512 * b:512 * b + 512],
                                      start=True, stop=True, tile_position=tp),
                             reads=qbk + kbk2, writes=[f"pb{bank}"],
                             tag=f"a{ph}s{p}_{i}{hk}{b}")
                    dbank = 4 * h + nb - 1
                    S.dve(lambda v, dbank=dbank, i=i: v.tensor_tensor(
                        out=pb[dbank][:], in0=pb[dbank][:],
                        in1=masks[:, 512 * (i % 4):512 * (i % 4) + 512],
                        op=ALU.add),
                        reads=[f"pb{dbank}", "masks"], writes=[f"pb{dbank}"],
                        tag=f"a{ph}m{p}_{i}{hk}")
                    mc = SC["mA"] if h == 0 else SC["mB"]
                    ncol = SC["nA"] if h == 0 else SC["nB"]
                    zc = SC["zA"] if h == 0 else SC["zB"]
                    zt = SC["zt"]
                    for b in range(nb):
                        bank = 4 * h + b
                        dstc = mc if b == 0 else zt
                        S.dve(lambda v, bank=bank, dstc=dstc: v.tensor_reduce(
                            out=stats[:, dstc:dstc + 1], in_=pb[bank][:],
                            axis=AX.X, op=ALU.max),
                            reads=[f"pb{bank}", f"st:{dstc}"],
                            writes=[f"st:{dstc}"], tag=f"a{ph}x{p}_{i}{hk}{b}")
                        if b > 0:
                            S.dve(lambda v, mc=mc, zt=zt: v.tensor_tensor(
                                out=stats[:, mc:mc + 1],
                                in0=stats[:, mc:mc + 1],
                                in1=stats[:, zt:zt + 1], op=ALU.max),
                                reads=[f"st:{mc}", f"st:{zt}"],
                                writes=[f"st:{mc}"],
                                tag=f"a{ph}xc{p}_{i}{hk}{b}")
                    S.dve(lambda v, mc=mc, ncol=ncol: v.tensor_scalar(
                        out=stats[:, ncol:ncol + 1], in0=stats[:, mc:mc + 1],
                        scalar1=-1.0, scalar2=None, op0=ALU.mult),
                        reads=[f"st:{mc}", f"st:{ncol}"], writes=[f"st:{ncol}"],
                        tag=f"a{ph}ng{p}_{i}{hk}")
                    for b in range(nb):
                        bank = 4 * h + b
                        zcol = zc if b == 0 else zt
                        S.act(lambda s, bank=bank, b=b, ncol=ncol, zcol=zcol:
                              s.activation(out=pbf[:, 512 * b:512 * b + 512],
                                           in_=pb[bank][:], func=ACTF.Exp,
                                           bias=stats[:, ncol:ncol + 1],
                                           accum_out=stats[:, zcol:zcol + 1]),
                              reads=[f"pb{bank}", f"st:{ncol}", f"st:{zcol}"]
                              + pbk2, writes=pbk2 + [f"st:{zcol}"],
                              tag=f"a{ph}e{p}_{i}{hk}{b}")
                        if b > 0:
                            S.dve(lambda v, zc=zc, zt=zt: v.tensor_tensor(
                                out=stats[:, zc:zc + 1], in0=stats[:, zc:zc + 1],
                                in1=stats[:, zt:zt + 1], op=ALU.add),
                                reads=[f"st:{zc}", f"st:{zt}"],
                                writes=[f"st:{zc}"],
                                tag=f"a{ph}zc{p}_{i}{hk}{b}")
                    rc = SC["rA"] if h == 0 else SC["rB"]
                    S.dve(lambda v, zc=zc, rc=rc: v.reciprocal(
                        out=stats[:, rc:rc + 1], in_=stats[:, zc:zc + 1]),
                        reads=[f"st:{zc}", f"st:{rc}"], writes=[f"st:{rc}"],
                        tag=f"a{ph}rz{p}_{i}{hk}")
                    S.dve(lambda v, i=i, rc=rc: v.tensor_scalar(
                        out=pbf[:, 0:128 * (i + 1)], in0=pbf[:, 0:128 * (i + 1)],
                        scalar1=stats[:, rc:rc + 1], scalar2=None, op0=ALU.mult),
                        reads=pbk2 + [f"st:{rc}"], writes=pbk2,
                        tag=f"a{ph}nm{p}_{i}{hk}")
                    ptbuf, ptk = (pta, ptak) if h == 0 else (ptb, ptbk)
                    for j in range(i + 1):
                        bank = 4 * h + j % 2
                        S.pe(lambda t, j=j, bank=bank: t.transpose(
                            pb[bank][:, 0:128], pbf[:, 128 * j:128 * j + 128],
                            ident[:]),
                            reads=pbk2 + ["ident"], writes=[f"pb{bank}"],
                            tag=f"a{ph}pt{p}_{i}{hk}{j}")
                        S.dve(lambda v, j=j, bank=bank, ptbuf=ptbuf:
                              v.tensor_copy(
                                  out=ptbuf[:, 128 * j:128 * j + 128],
                                  in_=pb[bank][:, 0:128]),
                              reads=[f"pb{bank}"] + ptk, writes=ptk,
                              tag=f"a{ph}pe{p}_{i}{hk}{j}")

                def pv_a(t, i=i):
                    last = None
                    for j in range(i + 1):
                        last = t.matmul(pb[2][0:64, 0:128],
                                        vt[:, 128 * j:128 * j + 64],
                                        pta[:, 128 * j:128 * j + 128],
                                        start=(j == 0), stop=(j == i))
                    return last

                def pv_b(t, i=i):
                    last = None
                    for j in range(i + 1):
                        last = t.matmul(pb[3][64:128, 0:128],
                                        vt[:, 128 * j + 64:128 * j + 128],
                                        ptb[:, 128 * j:128 * j + 128],
                                        start=(j == 0), stop=(j == i),
                                        tile_position=(0, 64))
                    return last
                S.pe(pv_a, reads=vtk + ptak, writes=["pb2"],
                     tag=f"a{ph}pva{p}_{i}")
                S.pe(pv_b, reads=vtk + ptbk, writes=["pb3"],
                     tag=f"a{ph}pvb{p}_{i}")
                S.dve(lambda v, i=i: v.tensor_copy(
                    out=vb[0:64, 128 * i:128 * i + 128], in_=pb[2][0:64, 0:128]),
                    reads=["pb2"] + vbk, writes=vbk, tag=f"a{ph}oa{p}_{i}")
                S.dve(lambda v, i=i: v.tensor_copy(
                    out=vb[64:128, 128 * i:128 * i + 128],
                    in_=pb[3][64:128, 0:128]),
                    reads=["pb3"] + vbk, writes=vbk, tag=f"a{ph}ob{p}_{i}")

            wod, wodk = wsl[1]
            wsrc = scr[wo].rearrange("b (g q) f -> q b g f", q=128)[:, :, p]
            S.dma(wod.rearrange("pp (b f) -> pp b f", b=CH), wsrc,
                  reads=[f"scr:{wo}:{b}" for b in range(CH)], writes=wodk,
                  tag=f"a{ph}wo{p}")
            for n in range(4):
                for d in range(CH):
                    bank = 6 + d % 2
                    S.pe(lambda t, d=d, n=n, bank=bank: t.matmul(
                        pb[bank][:], wod[:, 128 * d:128 * d + 128],
                        vb[:, 512 * n:512 * n + 512], start=True, stop=True),
                        reads=wodk + vbk, writes=[f"pb{bank}"],
                        tag=f"a{ph}wm{p}_{n}_{d}")
                    if layer == 1 and p == 0:
                        S.dve(lambda v, d=d, bank=bank: v.tensor_scalar(
                            out=pb[bank][:], in0=pb[bank][:],
                            scalar1=biasv[:, 24 + d:25 + d], scalar2=None,
                            op0=ALU.add),
                            reads=[f"pb{bank}", "biasv"], writes=[f"pb{bank}"],
                            tag=f"a{ph}bo{p}_{n}_{d}")
                    S.dve(lambda v, d=d, n=n, bank=bank: v.tensor_tensor(
                        out=xres[:, T * d + 512 * n: T * d + 512 * n + 512],
                        in0=xres[:, T * d + 512 * n: T * d + 512 * n + 512],
                        in1=pb[bank][:], op=ALU.add),
                        reads=[f"pb{bank}", f"xres:{d}"], writes=[f"xres:{d}"],
                        tag=f"a{ph}ac{p}_{n}_{d}")

    def emit_ffn1(ph):
        emit_ln(ph)
        amx, amxk = A(14336, 512)
        s127, s127k = A(14848, 512)
        tq, tqk = A(15360, 512)
        tq2, tq2k = A(15872, 512)
        for n in range(4):
            xlk = [f"xln:{c}:{n}" for c in range(CH)]
            def xsl(c, n=n):
                return xln[:, XST * c + 1 + 512 * n: XST * c + 1 + 512 * n + 512]
            S.dve(lambda v, xsl=xsl: v.tensor_tensor(
                out=amx, in0=xsl(0), in1=xsl(1), op=ALU.max),
                reads=xlk + amxk, writes=amxk, tag=f"f1{ph}a0{n}")
            S.dve(lambda v, xsl=xsl: v.tensor_tensor(
                out=tq, in0=xsl(0), in1=xsl(1), op=ALU.min),
                reads=xlk + tqk, writes=tqk, tag=f"f1{ph}b0{n}")
            for c in range(2, CH):
                S.dve(lambda v, c=c, xsl=xsl: v.tensor_tensor(
                    out=amx, in0=xsl(c), in1=amx, op=ALU.max),
                    reads=xlk + amxk, writes=amxk, tag=f"f1{ph}a{c}{n}")
                S.dve(lambda v, c=c, xsl=xsl: v.tensor_tensor(
                    out=tq, in0=xsl(c), in1=tq, op=ALU.min),
                    reads=xlk + tqk, writes=tqk, tag=f"f1{ph}b{c}{n}")
            S.dve(lambda v: v.scalar_tensor_tensor(
                out=amx, in0=tq, scalar=-1.0, in1=amx, op0=ALU.mult,
                op1=ALU.max),
                reads=tqk + amxk, writes=amxk, tag=f"f1{ph}aZ{n}")
            S.gpc(lambda g: g.partition_all_reduce(s127, amx, channels=128,
                                                   reduce_op=ReduceOp.max),
                  reads=amxk + s127k + ["lib:mlp"], writes=s127k,
                  tag=f"f1{ph}ar{n}")
            S.dve(lambda v: v.tensor_scalar(out=s127, in0=s127, scalar1=1e-5,
                                            scalar2=None, op0=ALU.max),
                  reads=s127k, writes=s127k, tag=f"f1{ph}cl{n}")
            S.dve(lambda v: v.tensor_copy(out=tq2, in_=s127),
                  reads=s127k + tq2k, writes=tq2k, tag=f"f1{ph}kp{n}")
            S.dve(lambda v: v.reciprocal(out=s127, in_=s127),
                  reads=s127k, writes=s127k, tag=f"f1{ph}rc{n}")
            S.dve(lambda v: v.tensor_scalar(out=s127, in0=s127, scalar1=127.0,
                                            scalar2=None, op0=ALU.mult),
                  reads=s127k, writes=s127k, tag=f"f1{ph}sc{n}")
            for u in range(4):
                bank = 6 + u % 2
                col = SC["bx"] + 4 * n + u
                S.pe(lambda t, u=u, bank=bank: t.transpose(
                    pb[bank][:, 0:128], tq2[:, 128 * u:128 * u + 128], ident[:]),
                    reads=tq2k + ["ident"], writes=[f"pb{bank}"],
                    tag=f"f1{ph}bt{n}_{u}")
                S.dve(lambda v, col=col, bank=bank: v.tensor_copy(
                    out=stats[:, col:col + 1], in_=pb[bank][:, 0:1]),
                    reads=[f"pb{bank}", f"st:{col}"], writes=[f"st:{col}"],
                    tag=f"f1{ph}bc{n}_{u}")
            for c in range(CH):
                src = xln[:, XST * c + 1 + 512 * n: XST * c + 1 + 512 * n + 512]
                dstq = xqv[:, 2 * (XST * c + 1 + 512 * n):
                           2 * (XST * c + 1 + 512 * n) + 512]
                S.dve(lambda v, src=src: v.tensor_tensor(
                    out=tq, in0=src, in1=s127, op=ALU.mult),
                    reads=xlk + s127k + tqk, writes=tqk, tag=f"f1{ph}q1{n}_{c}")
                S.dve(lambda v: v.tensor_scalar(
                    out=tq, in0=tq, scalar1=0.5, scalar2=None, op0=ALU.add),
                    reads=tqk, writes=tqk, tag=f"f1{ph}q2{n}_{c}")
                S.dve(lambda v, src=src: v.tensor_copy(
                    out=src.bitcast(I32), in_=tq),
                    reads=tqk + xlk, writes=[f"xln:{c}:{n}"],
                    tag=f"f1{ph}q3{n}_{c}")
                S.dve(lambda v, src=src: v.tensor_copy(
                    out=src, in_=src.bitcast(I32)),
                    reads=[f"xln:{c}:{n}"], writes=[f"xln:{c}:{n}"],
                    tag=f"f1{ph}q4{n}_{c}")
                S.dve(lambda v, src=src: v.tensor_tensor(
                    out=tq, in0=src, in1=tq, op=ALU.subtract),
                    reads=tqk + [f"xln:{c}:{n}"], writes=tqk,
                    tag=f"f1{ph}q5{n}_{c}")
                S.dve(lambda v: v.tensor_scalar(
                    out=tq, in0=tq, scalar1=0.0, scalar2=None, op0=ALU.is_gt),
                    reads=tqk, writes=tqk, tag=f"f1{ph}q6{n}_{c}")
                S.dve(lambda v, src=src: v.tensor_tensor(
                    out=tq, in0=src, in1=tq, op=ALU.subtract),
                    reads=tqk + [f"xln:{c}:{n}"], writes=tqk,
                    tag=f"f1{ph}q7{n}_{c}")
                S.dve(lambda v, dstq=dstq: v.tensor_scalar(
                    out=dstq, in0=tq, scalar1=127.0,
                    scalar2=-128.0, op0=ALU.min, op1=ALU.max),
                    reads=tqk + [f"xln:{c}:{n}"],
                    writes=[f"xq:{c}:{n}", f"xln:{c}:{n}"],
                    tag=f"f1{ph}q8{n}_{c}")

        gbuf, gbk = A(0, 8192)
        hqt, hqtk = AB(12288, 8192)
        gsl = (masks[:, 0:2048], ["masks"])
        usl, uslk = AB(16384, 2048)
        gst, gstk = A(10496, 256)
        tqh, tqhk = A(10752, 512)
        tqh2, tqh2k = A(11264, 512)
        dsl = [AB(8192, 1024), AB(8704, 1024)]
        bbc, bbck = A(10240, 256)

        xq_of = {}
        for c in range(CH):
            for n in range(4):
                xq_of[(c, n)] = 2 * (XST * c + 1 + 512 * n)

        def xq_st(c, tglob):
            n, r = divmod(tglob, 4)
            off = xq_of[(c, n)] + 128 * r
            return xqv[:, off:off + 128]

        scrap = {}
        for c in range(CH):
            for par in range(2):
                scrap[(c, par)] = 2 * (XST * c + 1 + 512 * par) + 512

        for ti in range(T // 256):
            sub0, sub1 = 2 * ti, 2 * ti + 1
            xqkeys = [f"xq:{c}:{n}" for c in range(CH) for n in range(4)]
            for s2 in range(2):
                tglob = 2 * ti + s2
                bxc = SC["bx"] + tglob
                agc, auc = SC["ag"] + 2 * s2, SC["au"] + 2 * s2
                S.dve(lambda v, bxc=bxc, agc=agc: v.tensor_tensor(
                    out=stats[:, agc:agc + 1], in0=stats[:, bxc:bxc + 1],
                    in1=stats[:, SC['Wg_m']:SC['Wg_m'] + 1], op=ALU.mult),
                    reads=[f"st:{bxc}", f"st:{SC['Wg_m']}", f"st:{agc}"],
                    writes=[f"st:{agc}"], tag=f"f1{ph}ag{ti}_{s2}")
                S.dve(lambda v, bxc=bxc, auc=auc: v.tensor_tensor(
                    out=stats[:, auc:auc + 1], in0=stats[:, bxc:bxc + 1],
                    in1=stats[:, SC['Wu_m']:SC['Wu_m'] + 1], op=ALU.mult),
                    reads=[f"st:{bxc}", f"st:{SC['Wu_m']}", f"st:{auc}"],
                    writes=[f"st:{auc}"], tag=f"f1{ph}au{ti}_{s2}")
            for it in range(16):
                pg = scr["L1f_Wg"].rearrange("b (c q) f -> q b c f", q=128)
                pu = scr["L1f_Wu"].rearrange("b (c q) f -> q b c f", q=128)
                gdst, gdk = gsl
                for bb in range(2):
                    dstv = gdst.rearrange("p (c b f) -> p b c f",
                                          c=CH, b=2)[:, bb]
                    S.dma(dstv, pg[:, 2 * it + bb],
                          reads=[f"scr:L1f_Wg:{2 * it + bb}"] + gdk,
                          writes=gdk, tag=f"f1{ph}gl{ti}_{it}_{bb}")
                    dstv2 = usl.rearrange("p (c b f) -> p b c f",
                                          c=CH, b=2)[:, bb]
                    S.dma(dstv2, pu[:, 2 * it + bb],
                          reads=[f"scr:L1f_Wu:{2 * it + bb}"] + uslk,
                          writes=uslk, tag=f"f1{ph}ul{ti}_{it}_{bb}")
                for s2 in range(2):
                    tglob = 2 * ti + s2
                    bnkg, bnku = 2 * s2, 2 * s2 + 1

                    def gmg(t, tglob=tglob, bnkg=bnkg, gdst=gdst):
                        last = None
                        for c in range(CH):
                            last = t.matmul(
                                pb[bnkg][:, 0:256], xq_st(c, tglob),
                                gdst[:, 256 * c:256 * c + 256],
                                start=(c == 0), stop=(c == CH - 1))
                        return last

                    def gmu(t, tglob=tglob, bnku=bnku):
                        last = None
                        for c in range(CH):
                            last = t.matmul(
                                pb[bnku][:, 0:256], xq_st(c, tglob),
                                usl[:, 256 * c:256 * c + 256],
                                start=(c == 0), stop=(c == CH - 1))
                        return last
                    S.pe(gmg, reads=gdk + xqkeys, writes=[f"pb{bnkg}"],
                         tag=f"f1{ph}gm{ti}_{it}_{s2}")
                    S.pe(gmu, reads=list(uslk) + xqkeys, writes=[f"pb{bnku}"],
                         tag=f"f1{ph}um{ti}_{it}_{s2}")
                    agc, auc = SC["ag"] + 2 * s2, SC["au"] + 2 * s2
                    S.act(lambda s, bnkg=bnkg, agc=agc: s.activation(
                        out=gst, in_=pb[bnkg][:, 0:256], func=ACTF.Silu,
                        scale=stats[:, agc:agc + 1]),
                        reads=[f"pb{bnkg}", f"st:{agc}"] + gstk,
                        writes=gstk, tag=f"f1{ph}si{ti}_{it}_{s2}")
                    gslice = gbuf[:, 4096 * s2 + 256 * it:4096 * s2 + 256 * it + 256]
                    S.dve(lambda v, gslice=gslice, bnku=bnku, auc=auc:
                          v.scalar_tensor_tensor(
                              out=gslice, in0=gst,
                              scalar=stats[:, auc:auc + 1],
                              in1=pb[bnku][:, 0:256], op0=ALU.mult,
                              op1=ALU.mult),
                          reads=gstk + [f"pb{bnku}", f"st:{auc}"] + gbk,
                          writes=gbk, tag=f"f1{ph}g{ti}_{it}_{s2}")
            for s2 in range(2):
                tglob = 2 * ti + s2
                gsub = gbuf[:, 4096 * s2:4096 * s2 + 4096]
                gmc, shc, bhc = SC["gm"], SC["sh"], SC["bh"]
                S.dve(lambda v, gsub=gsub: v.tensor_reduce(
                    out=stats[:, gmc:gmc + 1], in_=gsub, axis=AX.X, op=ALU.max,
                    apply_absolute_value=True),
                    reads=gbk + [f"st:{gmc}"], writes=[f"st:{gmc}"],
                    tag=f"f1{ph}gm2{ti}_{s2}")
                S.dve(lambda v: v.tensor_scalar(
                    out=stats[:, gmc:gmc + 1], in0=stats[:, gmc:gmc + 1],
                    scalar1=1e-5, scalar2=None, op0=ALU.max),
                    reads=[f"st:{gmc}"], writes=[f"st:{gmc}"],
                    tag=f"f1{ph}gc{ti}_{s2}")
                S.dve(lambda v: v.reciprocal(out=stats[:, shc:shc + 1],
                                             in_=stats[:, gmc:gmc + 1]),
                      reads=[f"st:{gmc}", f"st:{shc}"], writes=[f"st:{shc}"],
                      tag=f"f1{ph}gr{ti}_{s2}")
                S.dve(lambda v: v.tensor_scalar(
                    out=stats[:, shc:shc + 1], in0=stats[:, shc:shc + 1],
                    scalar1=127.0, scalar2=None, op0=ALU.mult),
                    reads=[f"st:{shc}"], writes=[f"st:{shc}"],
                    tag=f"f1{ph}g7{ti}_{s2}")
                S.dve(lambda v: v.tensor_tensor(
                    out=stats[:, bhc:bhc + 1], in0=stats[:, gmc:gmc + 1],
                    in1=stats[:, SC['Wd_m']:SC['Wd_m'] + 1], op=ALU.mult),
                    reads=[f"st:{gmc}", f"st:{SC['Wd_m']}", f"st:{bhc}"],
                    writes=[f"st:{bhc}"], tag=f"f1{ph}bh{ti}_{s2}")
                S.dve(lambda v: v.tensor_scalar(
                    out=stats[:, bhc:bhc + 1], in0=stats[:, bhc:bhc + 1],
                    scalar1=0.5, scalar2=None, op0=ALU.mult),
                    reads=[f"st:{bhc}"], writes=[f"st:{bhc}"],
                    tag=f"f1{ph}b5{ti}_{s2}")
                S.pe(lambda t, s2=s2: t.transpose(
                    pb[6][0:1, 0:128], stats[:, bhc:bhc + 1], ident[:]),
                    reads=[f"st:{bhc}", "ident"], writes=["pb6"],
                    tag=f"f1{ph}btr{ti}_{s2}")
                S.dve(lambda v, s2=s2: v.tensor_copy(
                    out=stats[0:1, 256 + 128 * s2:256 + 128 * s2 + 128],
                    in_=pb[6][0:1, 0:128]),
                    reads=["pb6", f"st:brow{s2}"], writes=[f"st:brow{s2}"],
                    tag=f"f1{ph}brw{ti}_{s2}")
                for half in range(4):
                    gsl2 = gsub[:, 1024 * half:1024 * half + 1024]
                    c0 = 2 * half
                    dq0 = xqv[:, scrap[(c0, s2)]:scrap[(c0, s2)] + 512]
                    dq1 = xqv[:, scrap[(c0 + 1, s2)]:scrap[(c0 + 1, s2)] + 512]
                    S.dve(lambda v, gsl2=gsl2: v.tensor_scalar(
                        out=tqh, in0=gsl2[:, 0:512],
                        scalar1=stats[:, shc:shc + 1], scalar2=0.5,
                        op0=ALU.mult, op1=ALU.add),
                        reads=gbk + [f"st:{shc}"] + tqhk, writes=tqhk,
                        tag=f"f1{ph}h1{ti}_{s2}_{half}")
                    S.dve(lambda v: v.tensor_copy(
                        out=tqh2.bitcast(I32), in_=tqh),
                        reads=tqhk + tqh2k, writes=tqh2k,
                        tag=f"f1{ph}h2{ti}_{s2}_{half}")
                    S.dve(lambda v: v.tensor_copy(
                        out=tqh2, in_=tqh2.bitcast(I32)),
                        reads=tqh2k, writes=tqh2k,
                        tag=f"f1{ph}h2b{ti}_{s2}_{half}")
                    S.dve(lambda v: v.tensor_tensor(
                        out=tqh, in0=tqh2, in1=tqh, op=ALU.subtract),
                        reads=tqhk + tqh2k, writes=tqhk,
                        tag=f"f1{ph}h3{ti}_{s2}_{half}")
                    S.dve(lambda v: v.tensor_scalar(
                        out=tqh, in0=tqh, scalar1=0.0, scalar2=None,
                        op0=ALU.is_gt),
                        reads=tqhk, writes=tqhk,
                        tag=f"f1{ph}h3b2{ti}_{s2}_{half}")
                    S.dve(lambda v: v.tensor_tensor(
                        out=tqh, in0=tqh2, in1=tqh, op=ALU.subtract),
                        reads=tqhk + tqh2k, writes=tqhk,
                        tag=f"f1{ph}h3b3{ti}_{s2}_{half}")
                    S.dve(lambda v, dq0=dq0: v.tensor_scalar(
                        out=dq0, in0=tqh, scalar1=127.0,
                        scalar2=-128.0, op0=ALU.min, op1=ALU.max),
                        reads=tqhk + [f"scr0:{c0}:{s2}"],
                        writes=[f"scr0:{c0}:{s2}"],
                        tag=f"f1{ph}h4{ti}_{s2}_{half}")
                    S.dve(lambda v, gsl2=gsl2: v.tensor_scalar(
                        out=tqh, in0=gsl2[:, 512:1024],
                        scalar1=stats[:, shc:shc + 1], scalar2=0.5,
                        op0=ALU.mult, op1=ALU.add),
                        reads=gbk + [f"st:{shc}"] + tqhk, writes=tqhk,
                        tag=f"f1{ph}h5{ti}_{s2}_{half}")
                    S.dve(lambda v: v.tensor_copy(
                        out=tqh2.bitcast(I32), in_=tqh),
                        reads=tqhk + tqh2k, writes=tqh2k,
                        tag=f"f1{ph}h6{ti}_{s2}_{half}")
                    S.dve(lambda v: v.tensor_copy(
                        out=tqh2, in_=tqh2.bitcast(I32)),
                        reads=tqh2k, writes=tqh2k,
                        tag=f"f1{ph}h6b{ti}_{s2}_{half}")
                    S.dve(lambda v: v.tensor_tensor(
                        out=tqh, in0=tqh2, in1=tqh, op=ALU.subtract),
                        reads=tqhk + tqh2k, writes=tqhk,
                        tag=f"f1{ph}h7{ti}_{s2}_{half}")
                    S.dve(lambda v: v.tensor_scalar(
                        out=tqh, in0=tqh, scalar1=0.0, scalar2=None,
                        op0=ALU.is_gt),
                        reads=tqhk, writes=tqhk,
                        tag=f"f1{ph}h7b2{ti}_{s2}_{half}")
                    S.dve(lambda v: v.tensor_tensor(
                        out=tqh, in0=tqh2, in1=tqh, op=ALU.subtract),
                        reads=tqhk + tqh2k, writes=tqhk,
                        tag=f"f1{ph}h7b3{ti}_{s2}_{half}")
                    S.dve(lambda v, dq1=dq1: v.tensor_scalar(
                        out=dq1, in0=tqh, scalar1=127.0,
                        scalar2=-128.0, op0=ALU.min, op1=ALU.max),
                        reads=tqhk + [f"scr0:{c0 + 1}:{s2}"],
                        writes=[f"scr0:{c0 + 1}:{s2}"],
                        tag=f"f1{ph}h8{ti}_{s2}_{half}")
                for ic in range(32):
                    cblk, rblk = divmod(ic, 4)
                    srcq = xqv[:, scrap[(cblk, s2)] + 128 * rblk:
                               scrap[(cblk, s2)] + 128 * rblk + 128]
                    bank = 6 + ic % 2
                    S.pe(lambda t, srcq=srcq, bank=bank: t.transpose(
                        pb[bank][:, 0:64].bitcast(BF16)[:, 0:128], srcq,
                        identb[:]),
                        reads=[f"scr0:{cblk}:{s2}", "identb"],
                        writes=[f"pb{bank}"], tag=f"f1{ph}t{ti}_{s2}_{ic}")
                    S.dve(lambda v, ic=ic, s2=s2, bank=bank: v.tensor_copy(
                        out=hqt[:, 256 * ic + 128 * s2:256 * ic + 128 * s2 + 128],
                        in_=pb[bank][:, 0:64].bitcast(BF16)[:, 0:128]),
                        reads=[f"pb{bank}"] + hqtk, writes=hqtk,
                        tag=f"f1{ph}te{ti}_{s2}_{ic}")
            S.gpc(lambda g: g.partition_broadcast(bbc, stats[0:1, 256:512]),
                  reads=["st:brow0", "st:brow1", "lib:mlp"] + bbck, writes=bbck,
                  tag=f"f1{ph}bb{ti}")
            pd = scr["L1f_Wd"].rearrange("b (i q) f -> q b i f", q=128)
            for ic in range(32):
                dv, dvk = dsl[ic % 2]
                S.dma(dv.rearrange("p (b f) -> p b f", b=CH), pd[:, :, ic],
                      reads=[f"scr:L1f_Wd:{b}" for b in range(CH)] + dvk,
                      writes=dvk, tag=f"f1{ph}dl{ti}_{ic}")

                def dmm(t, ic=ic, dv=dv):
                    last = None
                    for b in range(CH):
                        last = t.matmul(
                            pb[b][:, 0:256], dv[:, 128 * b:128 * b + 128],
                            hqt[:, 256 * ic:256 * ic + 256],
                            start=(ic == 0), stop=(ic == 31))
                    return last
                S.pe(dmm, reads=dvk + hqtk,
                     writes=[f"pb{b}" for b in range(CH)],
                     tag=f"f1{ph}dm{ti}_{ic}")
            for b in range(CH):
                S.dve(lambda v, b=b: v.tensor_tensor(
                    out=pb[b][:, 0:256], in0=pb[b][:, 0:256], in1=bbc,
                    op=ALU.mult),
                    reads=[f"pb{b}"] + bbck,
                    writes=[f"pb{b}"], tag=f"f1{ph}ds{ti}_{b}")
                S.dve(lambda v, b=b, ti=ti: v.tensor_tensor(
                    out=xres[:, T * b + 256 * ti: T * b + 256 * ti + 256],
                    in0=xres[:, T * b + 256 * ti: T * b + 256 * ti + 256],
                    in1=pb[b][:, 0:256], op=ALU.add),
                    reads=[f"pb{b}", f"xres:{b}"], writes=[f"xres:{b}"],
                    tag=f"f1{ph}dr{ti}_{b}")

    if STAGE >= 1:
        for nm in ("L0f_Wr", "L0f_Wk", "L0f_Wv"):
            prep_f32(nm)
        emit_ffn0("A")
    if STAGE >= 2:
        for nm in ("L0a_Wq", "L0a_Wk", "L0a_Wv", "L0a_Wo"):
            prep_f32(nm)
        emit_attn("B", 0)
    if STAGE >= 3:
        emit_ffn0("C")
    if STAGE >= 4:
        S.gpc(lambda g: g.load_library(library_config.mlp),
              writes=["lib:mlp"], tag="libmlp")
        for nm in ("L1f_Wg", "L1f_Wu", "L1f_Wd"):
            prep_bitlinear(nm)
        emit_ffn1("D")
    if STAGE >= 5:
        emit_masks()
        for nm in ("L1a_W", "L1a_Wo"):
            prep_f32(nm)
        emit_attn("E", 1)
    if STAGE >= 6:
        emit_ffn1("F")

    yt = y_out.rearrange("(k p) d -> k p d", p=128)
    for k in range(16):
        sl, slk = A(1024 * (k % 2), 1024)
        for c in range(CH):
            bank = 4 + c % 2
            S.pe(lambda t, c=c, k=k, bank=bank: t.transpose(
                pb[bank][:, 0:128],
                xres[:, T * c + 128 * k: T * c + 128 * k + 128], ident[:]),
                reads=[f"xres:{c}", "ident"], writes=[f"pb{bank}"],
                tag=f"oyt{k}_{c}")
            S.dve(lambda v, sl=sl, c=c, bank=bank: v.tensor_copy(
                out=sl[:, 128 * c:128 * c + 128], in_=pb[bank][:, 0:128]),
                reads=[f"pb{bank}"] + slk, writes=slk, tag=f"oye{k}_{c}")
        S.dma(yt[k], sl, stream="GD", reads=slk, writes=[f"y:{k}"],
              tag=f"oys{k}")

    with contextlib.ExitStack() as ctx:
        sems = {s: ctx.enter_context(nc.semaphore(f"sem_{s}"))
                for s in ("PE", "DVE", "ACT", "GPC")}
        pools = {s: [ctx.enter_context(nc.semaphore(f"sem_{s}{j}"))
                     for j in range(S.POOL[s])] for s in ("SD", "GD")}
        block = ctx.enter_context(nc.Block())
        S.finalize(block, sems, pools)
    mybir.codegen_inst_isa_subclasses(nc)
    return nc


def kernel(**inputs):
    xs = np.ascontiguousarray(np.asarray(inputs["L0_attn_in_features"],
                                         np.float32))
    wmap = {
        "L0f_Wr": "L0_ffn_W_r", "L0f_Wk": "L0_ffn_W_k", "L0f_Wv": "L0_ffn_W_v",
        "L0a_Wq": "L0_attn_Wq", "L0a_Wk": "L0_attn_Wk", "L0a_Wv": "L0_attn_Wv",
        "L0a_Wo": "L0_attn_Wo", "L1a_W": "L1_attn_w_attn",
        "L1a_Wo": "L1_attn_w_o", "L1f_Wg": "L1_ffn_w_gate",
        "L1f_Wu": "L1_ffn_w_up", "L1f_Wd": "L1_ffn_w_down",
        "mu_r": "L0_ffn_mu_r", "mu_k": "L0_ffn_mu_k",
        "b_attn": "L1_attn_b_attn", "b_o": "L1_attn_b_o",
    }
    shared = {k: np.ascontiguousarray(np.asarray(inputs[v], np.float32))
              for k, v in wmap.items()}
    nc = build()
    in_maps = [dict(shared, x=np.ascontiguousarray(xs[i])) for i in range(8)]
    res = run_bass_kernel_spmd(nc, in_maps, core_ids=list(range(8)))
    return np.stack([np.asarray(res.results[i]["y"]) for i in range(8)],
                    axis=0)
